# version 02
# baseline (speedup 1.0000x reference)
import numpy as np
import ml_dtypes
import concourse.bass as bass
import concourse.mybir as mybir
from concourse.alu_op_type import AluOpType as A
from concourse.bass_utils import run_bass_kernel_spmd
from contextlib import ExitStack

F32 = mybir.dt.float32
BF16 = mybir.dt.bfloat16
F16 = mybir.dt.float16
U16 = mybir.dt.uint16
I32 = mybir.dt.int32
AF = mybir.ActivationFunctionType

S = 4096
D = 2048
NH = 8
DQ = 128
DV = 256
QK = 1024
L = 128
NCH = S // L
EPS_LN = 1e-5
EPS_MH = 1e-6
EPS_RMS = 1e-6
CAP = 15.0
I_FFN = 5632
ALL_PHASES = ("prep", "p1a", "p1b", "p0", "p23", "p4", "p5")


class Sem:
    def __init__(self, ctx, nc, name):
        self.h = ctx.enter_context(nc.semaphore(name))
        self.n = 0

    def inc(self, instr, k=1):
        instr.then_inc(self.h, k)
        self.n += k
        return self.n


def build_nc(dbg=False, phases=ALL_PHASES, p1a_mode="full"):
    nc = bass.Bass("TRN2", target_bir_lowering=False)
    di = lambda name, shape, dt: nc.dram_tensor(name, shape, dt, kind="ExternalInput").ap()
    dint = lambda name, shape, dt: nc.dram_tensor(name, shape, dt, kind="Internal").ap()

    def dout(name, shape, dt):
        return nc.dram_tensor(name, shape, dt, kind="ExternalOutput").ap()

    ddbg = dout if dbg else dint

    x = di("x", [S, D], F32)
    wq = di("wq", [QK, D], F32)
    wk = di("wk", [QK, D], F32)
    wv = di("wv", [D, D], F32)
    wog = di("wog", [D, D], F32)
    wig = di("wig", [NH, D], F32)
    wfg = di("wfg", [NH, D], F32)
    big = di("big", [NH, 1], F32)
    bfg = di("bfg", [NH, 1], F32)
    gamma = di("gamma", [NH, DV], F32)
    wout = di("wout", [D, D], F32)
    gup = di("gup", [2 * I_FFN, 512], U16)
    gsc = di("gsc", [16, 2 * I_FFN], F16)
    dwp = di("dwp", [D, 1408], U16)
    dsc = di("dsc", [44, D], F16)
    idb = di("idb", [128, 128], BF16)
    idf = di("idf", [128, 128], F32)
    maskb = di("maskb", [128, 128], BF16)
    onesb = di("onesb", [128, 128], BF16)
    onesf = di("onesf", [128, 128], F32)
    zerof = di("zerof", [NH, L], F32)
    zero2 = di("zero2", [128, DV], F32)

    out = dout("out", [S, D], F32)
    wqbf = dint("wqbf", [QK, D], BF16)
    wkbf = dint("wkbf", [QK, D], BF16)
    wvbf = dint("wvbf", [D, D], BF16)
    wogbf = dint("wogbf", [D, D], BF16)
    woutbf = dint("woutbf", [D, D], BF16)
    wifbf = dint("wifbf", [2 * NH, D], BF16)
    wguT_d = ddbg("wguT_d", [16 * 128, 2 * I_FFN], BF16)
    wdnT_d = ddbg("wdnT_d", [44 * 128, D], BF16)
    H_d = ddbg("H_d", [I_FFN, S], BF16)
    y1_d = ddbg("y1_d", [S, 1024], F32)
    xlnd = ddbg("xlnd", [S, D], BF16)
    xpd = ddbg("xpd", [S, D], F32)
    xlnpd = ddbg("xlnpd", [S, D], BF16)
    hobd = ddbg("hobd", [S, D], BF16)
    dbg_qn = ddbg("dbg_qn", [128, NH], F32)
    dbg_gt = ddbg("dbg_gt", [128, 24], F32)
    dbg_dinv = ddbg("dbg_dinv", [128, NH], F32)
    dbg_hcat = ddbg("dbg_hcat", [128, D], BF16)
    dbg_stats = ddbg("dbg_stats", [128, 4 * NH], F32)
    dbg_ysb = ddbg("dbg_ysb", [128, D], F32)
    dbg_t5 = ddbg("dbg_t5", [32 * 128, 16], F32)
    qd = ddbg("qd", [S, QK], BF16)
    kd = ddbg("kd", [S, QK], BF16)
    ifd = ddbg("ifd", [S, 2 * NH], F32)
    vd = ddbg("vd", [S, D], BF16)
    ogd = ddbg("ogd", [S, D], BF16)

    with ExitStack() as octx:
        def sb(ctx, name, shape, dt):
            return ctx.enter_context(nc.sbuf_tensor(name, shape, dt))

        c_idb = sb(octx, "c_idb", [128, 128], BF16)
        c_idf = sb(octx, "c_idf", [128, 128], F32)
        c_mask = sb(octx, "c_mask", [128, 128], BF16)
        c_ones = sb(octx, "c_ones", [128, 128], BF16)
        c_onesf = sb(octx, "c_onesf", [128, 128], F32)
        c_zero = sb(octx, "c_zero", [NH, L], F32)
        s_chain = Sem(octx, nc, "chain")

        def vg(eng, instr):
            s_chain.inc(instr)
            eng.wait_ge(s_chain.h, s_chain.n)
            return instr

        if "prep" in phases:
            with ExitStack() as ctx:
                stg = [sb(ctx, f"p_stg{i}", [128, 8 * D], BF16) for i in range(2)]
                s_ld = [Sem(octx, nc, f"p_ld{i}") for i in range(2)]
                s_st = [Sem(octx, nc, f"p_st{i}") for i in range(2)]

                blk = ctx.enter_context(nc.Block())

                wifs = sb(ctx, "p_wifs", [2 * NH, D], F32)
                wifb = sb(ctx, "p_wifb", [2 * NH, D], BF16)
                s_if = Sem(octx, nc, "p_if")
                s_ifc = Sem(octx, nc, "p_ifc")

                @blk.sync
                def _(sync):
                    for dst, src in [(c_idb, idb), (c_idf, idf), (c_mask, maskb),
                                     (c_ones, onesb), (c_onesf, onesf), (c_zero, zerof)]:
                        s_if.inc(sync.dma_start(out=dst[:], in_=src), 16)
                    s_if.inc(sync.dma_start(out=wifs[0:NH, :], in_=wig), 16)
                    s_if.inc(sync.dma_start(out=wifs[NH:2 * NH, :], in_=wfg), 16)

                @blk.vector
                def _(vector):
                    vector.wait_ge(s_if.h, 8 * 16)
                    s_ifc.inc(vector.tensor_copy(wifb[:], wifs[:]))

                @blk.scalar
                def _(scalar):
                    scalar.wait_ge(s_ifc.h, 1)
                    s_if.inc(scalar.dma_start(out=wifbf, in_=wifb[:]), 16)

                groups = []
                for (src, dst, rows) in [(wq, wqbf, QK), (wk, wkbf, QK), (wv, wvbf, D),
                                         (wog, wogbf, D), (wout, woutbf, D)]:
                    for r0 in range(0, rows, QK):
                        groups.append((src[r0:r0 + QK, :], dst[r0:r0 + QK, :]))

                @blk.gpsimd
                def _(g):
                    for gi, (srcv, dstv) in enumerate(groups):
                        p = gi % 2
                        if gi >= 2:
                            g.wait_ge(s_st[p].h, 16 * (gi // 2))
                        s_ld[p].inc(g.dma_start(out=stg[p][:],
                                                in_=srcv.rearrange("(c p) d -> p c d", p=128)), 16)

                @blk.sync
                def _(sync):
                    for gi, (srcv, dstv) in enumerate(groups):
                        p = gi % 2
                        sync.wait_ge(s_ld[p].h, 16 * (gi // 2 + 1))
                        s_st[p].inc(sync.dma_start(out=dstv.rearrange("(c p) d -> p c d", p=128),
                                                   in_=stg[p][:]), 16)
                    sync.wait_ge(s_st[0].h, s_st[0].n)
                    sync.wait_ge(s_st[1].h, s_st[1].n)
                    sync.wait_ge(s_if.h, s_if.n)

        if "p1a" in phases:
            with ExitStack() as ctx:
                sx = [sb(ctx, f"a_x{i}", [128, D], F32) for i in range(2)]
                xln = [sb(ctx, f"a_xln{i}", [128, D], BF16) for i in range(2)]
                xlnT = [sb(ctx, f"a_xlnT{i}", [128, D], BF16) for i in range(2)]
                jnk = sb(ctx, "a_jnk", [128, D], BF16)
                jnk2 = sb(ctx, "a_jnk2", [128, D], BF16)
                sum1 = sb(ctx, "a_sum1", [128, 1], F32)
                sum2 = sb(ctx, "a_sum2", [128, 1], F32)
                mean = sb(ctx, "a_mean", [128, 1], F32)
                varv = sb(ctx, "a_var", [128, 1], F32)
                m2 = sb(ctx, "a_m2", [128, 1], F32)
                rstd = sb(ctx, "a_rstd", [128, 1], F32)
                wqT = sb(ctx, "a_wqT", [128, 16 * QK], BF16)
                wkT = sb(ctx, "a_wkT", [128, 16 * QK], BF16)
                wifT = sb(ctx, "a_wifT", [128, 16 * 2 * NH], BF16)
                qbf = sb(ctx, "a_qbf", [128, QK], BF16)
                kbf = sb(ctx, "a_kbf", [128, QK], BF16)
                ifo = sb(ctx, "a_ifo", [128, 2 * NH], F32)
                p_t = [ctx.enter_context(nc.psum_tensor(f"a_pt{i}", [128, 128], BF16)) for i in range(2)]
                p_q = [ctx.enter_context(nc.psum_tensor(f"a_pq{i}", [128, 512], F32)) for i in range(2)]
                p_k = [ctx.enter_context(nc.psum_tensor(f"a_pk{i}", [128, 512], F32)) for i in range(2)]
                p_if = ctx.enter_context(nc.psum_tensor("a_pif", [128, 2 * NH], F32))

                s_x = [Sem(octx, nc, f"a_sx{i}") for i in range(2)]
                s_tw = Sem(octx, nc, "a_stw")
                s_var = Sem(octx, nc, "a_svar")
                s_sq = s_var
                s_ln = Sem(octx, nc, "a_sln")
                s_pt = Sem(octx, nc, "a_spt")
                s_te = Sem(octx, nc, "a_ste")
                s_pj = Sem(octx, nc, "a_spj")
                s_ev = Sem(octx, nc, "a_sev")
                s_stx = Sem(octx, nc, "a_sstx")
                s_stq = Sem(octx, nc, "a_sstq")
                NSTQ = {"full": 48, "proj2": 32, "proj1": 16}.get(p1a_mode, 0)
                blk = ctx.enter_context(nc.Block())

                @blk.scalar
                def _(scalar):
                    if p1a_mode in ("lntr3", "lntr4"):
                        return
                    for kk in range(16):
                        s_tw.inc(scalar.dma_start_transpose(
                            out=wqT[:, kk * QK:(kk + 1) * QK],
                            in_=wqbf[:, kk * 128:(kk + 1) * 128]), 16)
                    for kk in range(16):
                        s_tw.inc(scalar.dma_start_transpose(
                            out=wkT[:, kk * QK:(kk + 1) * QK],
                            in_=wkbf[:, kk * 128:(kk + 1) * 128]), 16)
                    for kk in range(16):
                        s_tw.inc(scalar.dma_start_transpose(
                            out=wifT[:, kk * 16:(kk + 1) * 16],
                            in_=wifbf[:, kk * 128:(kk + 1) * 128]), 16)
                    scalar.wait_ge(s_tw.h, s_tw.n)
                NTW = s_tw.n

                @blk.sync
                def _(sync):
                    sync.wait_ge(s_tw.h, NTW)
                    for c in range(NCH):
                        if c >= 2:
                            sync.wait_ge(s_ln.h, c - 1)
                        s_x[c % 2].inc(sync.dma_start(out=sx[c % 2][:],
                                                      in_=x[c * L:(c + 1) * L, :]), 16)

                @blk.scalar
                def _(scalar):
                    for c in range(NCH):
                        scalar.wait_ge(s_x[c % 2].h, 16 * (c // 2 + 1))
                        if c >= 1:
                            scalar.wait_ge(s_var.h, 3 * c - 1)
                        scalar.activation(out=jnk[:], in_=sx[c % 2][:], func=AF.Square,
                                          accum_out=sum2[:])
                        s_sq.inc(scalar.activation(out=jnk2[:], in_=sx[c % 2][:], func=AF.Identity,
                                                   accum_out=sum1[:]))
                        scalar.wait_ge(s_var.h, 3 * c + 2)
                        scalar.activation(out=rstd[:], in_=varv[:], func=AF.Ln)
                        s_var.inc(scalar.activation(out=rstd[:], in_=rstd[:], func=AF.Exp, scale=-0.5))
                        if p1a_mode not in ("ln", "lntr4"):
                            if c >= 2:
                                scalar.wait_ge(s_pj.h, c - 1)
                            for kk in range(16):
                                scalar.wait_ge(s_pt.h, 16 * c + kk + 1)
                                s_te.inc(scalar.copy(out=xlnT[c % 2][:, kk * 128:(kk + 1) * 128],
                                                     in_=p_t[kk % 2][:]))

                @blk.vector
                def _(vector):
                    for c in range(NCH):
                        vector.wait_ge(s_x[c % 2].h, 16 * (c // 2 + 1))
                        if c >= 2:
                            vector.wait_ge(s_stx.h, 16 * (c - 1))
                        vector.wait_ge(s_sq.h, c + 1)
                        vector.tensor_scalar(out=mean[:], in0=sum1[:], scalar1=1.0 / D,
                                             scalar2=None, op0=A.mult)
                        vector.tensor_scalar(out=varv[:], in0=sum2[:], scalar1=1.0 / D,
                                             scalar2=None, op0=A.mult)
                        vector.tensor_tensor(out=m2[:], in0=mean[:], in1=mean[:], op=A.mult)
                        vector.tensor_tensor(out=varv[:], in0=varv[:], in1=m2[:], op=A.subtract)
                        s_var.inc(vector.tensor_scalar(out=varv[:], in0=varv[:], scalar1=EPS_LN,
                                                       scalar2=None, op0=A.add))
                        vector.wait_ge(s_var.h, 3 * c + 3)
                        s_ln.inc(vector.tensor_scalar(out=xln[c % 2][:], in0=sx[c % 2][:],
                                                      scalar1=mean[:], scalar2=rstd[:],
                                                      op0=A.subtract, op1=A.mult))
                        if p1a_mode == "lntr4":
                            if c >= 2:
                                vector.wait_ge(s_pj.h, c - 1)
                            for kk in range(16):
                                vector.wait_ge(s_pt.h, 16 * c + kk + 1)
                                s_te.inc(vector.tensor_copy(xlnT[c % 2][:, kk * 128:(kk + 1) * 128],
                                                            p_t[kk % 2][:]))
                        if p1a_mode in ("full", "proj1", "proj2"):
                            vector.wait_ge(s_pj.h, c + 1)
                            if c >= 1:
                                vector.wait_ge(s_stq.h, NSTQ * c)
                            vector.tensor_copy(qbf[:, 0:512], p_q[0][:])
                            vector.tensor_copy(qbf[:, 512:1024], p_q[1][:])
                            if p1a_mode in ("full", "proj2"):
                                vector.tensor_copy(kbf[:, 0:512], p_k[0][:])
                                vector.tensor_copy(kbf[:, 512:1024], p_k[1][:])
                            if p1a_mode == "full":
                                vector.tensor_copy(ifo[:], p_if[:])
                            s_ev.inc(vector.engine_nop() if False else vector.tensor_copy(qbf[:, 0:1].bitcast(BF16), qbf[:, 0:1]) if False else vector.tensor_copy(jnk2[:, 0:1], qbf[:, 0:1]))

                @blk.tensor
                def _(tensor):
                    if p1a_mode == "ln":
                        return
                    if p1a_mode not in ("lntr2", "lntr3", "lntr4"):
                        tensor.wait_ge(s_tw.h, NTW)
                    for c in range(NCH):
                        tensor.wait_ge(s_ln.h, c + 1)
                        for kk in range(16):
                            if 16 * c + kk - 1 >= 1:
                                tensor.wait_ge(s_te.h, 16 * c + kk - 1)
                            s_pt.inc(tensor.transpose(p_t[kk % 2][:],
                                                      xln[c % 2][:, kk * 128:(kk + 1) * 128],
                                                      c_idb[:]))
                        if p1a_mode not in ("full", "proj1", "proj2"):
                            s_pj.inc(tensor.nop())
                            continue
                        tensor.wait_ge(s_ev.h, c)
                        for kk in range(16):
                            tensor.wait_ge(s_te.h, 16 * c + kk + 1)
                            lhsT = xlnT[c % 2][:, kk * 128:(kk + 1) * 128]
                            st = (kk == 0)
                            sp = (kk == 15)
                            tensor.matmul(p_q[0][:], lhsT, wqT[:, kk * QK:kk * QK + 512], start=st, stop=sp)
                            tensor.matmul(p_q[1][:], lhsT, wqT[:, kk * QK + 512:kk * QK + 1024], start=st, stop=sp)
                            if p1a_mode in ("full", "proj2"):
                                tensor.matmul(p_k[0][:], lhsT, wkT[:, kk * QK:kk * QK + 512], start=st, stop=sp)
                                tensor.matmul(p_k[1][:], lhsT, wkT[:, kk * QK + 512:kk * QK + 1024], start=st, stop=sp)
                            if p1a_mode == "full":
                                mmlast = tensor.matmul(p_if[:], lhsT, wifT[:, kk * 16:(kk + 1) * 16], start=st, stop=sp)
                        s_pj.inc(mmlast)

                @blk.gpsimd
                def _(g):
                    for c in range(NCH):
                        g.wait_ge(s_ln.h, c + 1)
                        s_stx.inc(g.dma_start(out=xlnd[c * L:(c + 1) * L, :], in_=xln[c % 2][:]), 16)
                        if p1a_mode in ("full", "proj1", "proj2"):
                            g.wait_ge(s_ev.h, c + 1)
                            s_stq.inc(g.dma_start(out=qd[c * L:(c + 1) * L, :], in_=qbf[:]), 16)
                            if p1a_mode in ("full", "proj2"):
                                s_stq.inc(g.dma_start(out=kd[c * L:(c + 1) * L, :], in_=kbf[:]), 16)
                            if p1a_mode == "full":
                                s_stq.inc(g.dma_start(out=ifd[c * L:(c + 1) * L, :], in_=ifo[:]), 16)
                    g.wait_ge(s_stx.h, s_stx.n)
                    g.wait_ge(s_stq.h, s_stq.n)

        if "p1b" in phases:
            with ExitStack() as ctx:
                wvT = sb(ctx, "b_wvT", [128, 16 * D], BF16)
                wogT = sb(ctx, "b_wogT", [128, 16 * D], BF16)
                xq = sb(ctx, "b_xq", [128, 16 * QK], BF16)
                vbf = sb(ctx, "b_vbf", [128, D], BF16)
                ogbf = sb(ctx, "b_ogbf", [128, D], BF16)
                p_v = ctx.enter_context(nc.psum_tensor("b_pv", [128, D], F32))
                p_og = ctx.enter_context(nc.psum_tensor("b_pog", [128, D], F32))
                s_w = Sem(octx, nc, "b_sw")
                s_xq = Sem(octx, nc, "b_sxq")
                s_pj = Sem(octx, nc, "b_spj")
                s_ev = Sem(octx, nc, "b_sev")
                s_st = Sem(octx, nc, "b_sst")
                blk = ctx.enter_context(nc.Block())

                @blk.scalar
                def _(scalar):
                    for kk in range(16):
                        s_w.inc(scalar.dma_start_transpose(
                            out=wvT[:, kk * D:(kk + 1) * D],
                            in_=wvbf[:, kk * 128:(kk + 1) * 128]), 16)
                    for kk in range(16):
                        s_w.inc(scalar.dma_start_transpose(
                            out=wogT[:, kk * D:(kk + 1) * D],
                            in_=wogbf[:, kk * 128:(kk + 1) * 128]), 16)
                    for tq in range(4):
                        if tq >= 1:
                            scalar.wait_ge(s_pj.h, 8 * tq)
                        for kk in range(16):
                            s_xq.inc(scalar.dma_start_transpose(
                                out=xq[:, kk * QK:(kk + 1) * QK],
                                in_=xlnd[tq * QK:(tq + 1) * QK, kk * 128:(kk + 1) * 128]), 16)
                NW1B = 32 * 16

                @blk.tensor
                def _(tensor):
                    tensor.wait_ge(s_w.h, NW1B)
                    for t in range(NCH):
                        tq, tc = divmod(t, 8)
                        tensor.wait_ge(s_xq.h, 256 * (tq + 1))
                        tensor.wait_ge(s_ev.h, t)
                        mml = None
                        for kk in range(16):
                            lhsT = xq[:, kk * QK + tc * 128:kk * QK + (tc + 1) * 128]
                            st = (kk == 0)
                            sp = (kk == 15)
                            for j in range(4):
                                tensor.matmul(p_v[:, j * 512:(j + 1) * 512], lhsT,
                                              wvT[:, kk * D + j * 512:kk * D + (j + 1) * 512],
                                              start=st, stop=sp)
                            for j in range(4):
                                mml = tensor.matmul(p_og[:, j * 512:(j + 1) * 512], lhsT,
                                                    wogT[:, kk * D + j * 512:kk * D + (j + 1) * 512],
                                                    start=st, stop=sp)
                        s_pj.inc(mml)

                @blk.vector
                def _(vector):
                    for t in range(NCH):
                        vector.wait_ge(s_pj.h, t + 1)
                        if t >= 1:
                            vector.wait_ge(s_st.h, 32 * t)
                        vector.tensor_copy(vbf[:], p_v[:])
                        s_ev.inc(vector.tensor_copy(ogbf[:], p_og[:]))

                @blk.gpsimd
                def _(g):
                    for t in range(NCH):
                        g.wait_ge(s_ev.h, t + 1)
                        s_st.inc(g.dma_start(out=vd[t * L:(t + 1) * L, :], in_=vbf[:]), 16)
                        s_st.inc(g.dma_start(out=ogd[t * L:(t + 1) * L, :], in_=ogbf[:]), 16)
                    g.wait_ge(s_st.h, s_st.n)



        if "p0" in phases:
            with ExitStack() as ctx:
                NSL = 8
                FG = 1408
                pkT = [sb(ctx, f"z_pkT{i}", [128, D], U16) for i in range(2)]
                screp = [sb(ctx, f"z_scr{i}", [128, D], BF16) for i in range(2)]
                tw = sb(ctx, "z_tw", [128, D], U16)
                qv = sb(ctx, "z_qv", [128, D], U16)
                pu = sb(ctx, "z_pu", [128, D], U16)
                sbv = sb(ctx, "z_sbv", [128, D], U16)
                sgn = sb(ctx, "z_sgn", [128, D], U16)
                g2 = sb(ctx, "z_g2", [128, D], BF16)
                vmin = sb(ctx, "z_vmin", [128, D], BF16)
                vsg = sb(ctx, "z_vsg", [128, D], U16)
                wout_t = [sb(ctx, f"z_wo{i}", [128, D], BF16) for i in range(8)]
                s_pk = [Sem(octx, nc, f"z_spk{i}") for i in range(2)]
                s_sc = [Sem(octx, nc, f"z_ssc{i}") for i in range(2)]
                s_dq = Sem(octx, nc, "z_sdq")
                s_wo = Sem(octx, nc, "z_swo")
                blk = ctx.enter_context(nc.Block())

                jobs = []
                for ub in range(4):
                    for sl in range(NSL):
                        jobs.append(("g", ub, sl))
                for ub in range(11):
                    jobs.append(("d", ub, 0))
                NJ = len(jobs)

                @blk.scalar
                def _(scalar):
                    for ji, (kind, ub, sl) in enumerate(jobs):
                        pp = ji % 2
                        if ji >= 2:
                            scalar.wait_ge(s_dq.h, 4 * (ji - 1))
                        if kind == "g":
                            src = gup[sl * FG:(sl + 1) * FG, ub * 128:(ub + 1) * 128]
                            s_pk[pp].inc(scalar.dma_start_transpose(
                                out=pkT[pp][:, 0:FG], in_=src), 16)
                        else:
                            src = dwp[:, ub * 128:(ub + 1) * 128]
                            s_pk[pp].inc(scalar.dma_start_transpose(
                                out=pkT[pp][:, 0:D], in_=src), 16)

                @blk.gpsimd
                def _(g):
                    for ji, (kind, ub, sl) in enumerate(jobs):
                        pp = ji % 2
                        if ji >= 2:
                            g.wait_ge(s_dq.h, 4 * (ji - 1))
                        if kind == "g":
                            rep = gsc[4 * ub:4 * ub + 4, sl * FG:(sl + 1) * FG] \
                                .unsqueeze(1).broadcast_to([4, 32, FG])
                            s_sc[pp].inc(g.dma_start(out=screp[pp][:, 0:FG], in_=rep,
                                                     allow_slow_non_contiguous=True), 16)
                        else:
                            rep = dsc[4 * ub:4 * ub + 4, :].unsqueeze(1).broadcast_to([4, 32, D])
                            s_sc[pp].inc(g.dma_start(out=screp[pp][:, 0:D], in_=rep,
                                                     allow_slow_non_contiguous=True), 16)

                @blk.sync
                def _(sync):
                    for ji, (kind, ub, sl) in enumerate(jobs):
                        for s4 in range(4):
                            sync.wait_ge(s_dq.h, 4 * ji + s4 + 1)
                            ot = wout_t[(ji % 2) * 4 + s4]
                            if kind == "g":
                                ck = 4 * s4 + ub
                                dst = wguT_d[128 * ck:128 * (ck + 1), sl * FG:(sl + 1) * FG]
                                s_wo.inc(sync.dma_start(out=dst, in_=ot[:, 0:FG]), 16)
                            else:
                                ik = 11 * s4 + ub
                                dst = wdnT_d[128 * ik:128 * (ik + 1), :]
                                s_wo.inc(sync.dma_start(out=dst, in_=ot[:, 0:D]), 16)
                    sync.wait_ge(s_wo.h, s_wo.n)

                @blk.vector
                def _(vector):
                    for ji, (kind, ub, sl) in enumerate(jobs):
                        pp = ji % 2
                        F = FG if kind == "g" else D
                        vector.wait_ge(s_pk[pp].h, 16 * (ji // 2 + 1))
                        vector.wait_ge(s_sc[pp].h, 16 * (ji // 2 + 1))
                        if ji >= 2:
                            vector.wait_ge(s_wo.h, 64 * (ji - 1))
                        pk_ = pkT[pp]
                        for s4 in range(4):
                            ot = wout_t[pp * 4 + s4]
                            if s4 == 0:
                                twv = pk_
                            else:
                                vector.tensor_scalar(out=tw[:, 0:F].bitcast(I32),
                                                     in0=pk_[:, 0:F].bitcast(I32),
                                                     scalar1=4 * s4, scalar2=None,
                                                     op0=A.logical_shift_right)
                                twv = tw
                            vector.tensor_scalar(out=qv[:, 0:F], in0=twv[:, 0:F], scalar1=7,
                                                 scalar2=None, op0=A.bitwise_and)
                            vector.tensor_scalar(out=pu[:, 0:F], in0=qv[:, 0:F], scalar1=252,
                                                 scalar2=64, op0=A.add, op1=A.mult)
                            vector.tensor_scalar(out=sbv[:, 0:F], in0=twv[:, 0:F], scalar1=8,
                                                 scalar2=None, op0=A.bitwise_and)
                            vector.tensor_scalar(out=sgn[:, 0:F], in0=sbv[:, 0:F], scalar1=4096,
                                                 scalar2=None, op0=A.mult)
                            vector.tensor_scalar(out=g2[:, 0:F], in0=pu[:, 0:F].bitcast(BF16),
                                                 scalar1=2.0, scalar2=-1.0, op0=A.mult, op1=A.add)
                            vector.tensor_tensor(out=vmin[:, 0:F], in0=g2[:, 0:F],
                                                 in1=pu[:, 0:F].bitcast(BF16), op=A.min)
                            vector.tensor_tensor(out=vsg[:, 0:F], in0=vmin[:, 0:F].bitcast(U16),
                                                 in1=sgn[:, 0:F], op=A.bitwise_or)
                            s_dq.inc(vector.tensor_tensor(out=ot[:, 0:F],
                                                          in0=vsg[:, 0:F].bitcast(BF16),
                                                          in1=screp[pp][:, 0:F], op=A.mult))

        if "p23" in phases:
            with ExitStack() as ctx:
                LNQK = float(np.log(DQ ** -0.5))
                q_in = [sb(ctx, f"c_q{i}", [128, QK], BF16) for i in range(2)]
                k_in = [sb(ctx, f"c_k{i}", [128, QK], BF16) for i in range(2)]
                if_in = [sb(ctx, f"c_if{i}", [128, 2 * NH], F32) for i in range(2)]
                v_in = [sb(ctx, f"c_v{i}", [128, D], BF16) for i in range(2)]
                og_in = [sb(ctx, f"c_og{i}", [128, D], BF16) for i in range(2)]
                x_in = [sb(ctx, f"c_x{i}", [128, D], F32) for i in range(2)]
                woutT = sb(ctx, "c_woutT", [128, 16 * D], BF16)
                gammaT = sb(ctx, "c_gammaT", [128, 16], F32)
                gf_sb = sb(ctx, "c_gf", [16, 128], F32)
                sbig = sb(ctx, "c_big", [NH, 1], F32)
                sbfg = sb(ctx, "c_bfg", [NH, 1], F32)
                C_st = sb(ctx, "c_Cst", [128, NH * DV], F32)
                n_st = sb(ctx, "c_nst", [128, NH], F32)
                m_st = sb(ctx, "c_mst", [NH, 1], F32)
                g_i1 = sb(ctx, "c_gi1", [NH, L], F32)
                g_f1 = sb(ctx, "c_gf1", [NH, L], F32)
                g_ti = sb(ctx, "c_gti", [NH, L], F32)
                g_tf = sb(ctx, "c_gtf", [NH, L], F32)
                g_ic = sb(ctx, "c_gic", [NH, L], F32)
                g_fc = sb(ctx, "c_gfc", [NH, L], F32)
                g_va = sb(ctx, "c_gva", [NH, L], F32)
                g_e = sb(ctx, "c_ge", [NH, L], F32)
                g_l = sb(ctx, "c_gl", [NH, L], F32)
                g_mn = sb(ctx, "c_gmn", [NH, L], F32)
                g_fl = sb(ctx, "c_gfl", [NH, L], F32)
                g_b = sb(ctx, "c_gb", [NH, L], F32)
                g_m = sb(ctx, "c_gm", [NH, L], F32)
                g_a = sb(ctx, "c_ga", [NH, L], F32)
                g_t1 = sb(ctx, "c_gt1", [NH, L], F32)
                g_e1 = sb(ctx, "c_ge1", [NH, L], F32)
                g_ea = sb(ctx, "c_gea", [NH, L], F32)
                g_em = sb(ctx, "c_gem", [NH, L], F32)
                g_amax = sb(ctx, "c_gamax", [NH, 1], F32)
                g_amln = sb(ctx, "c_gamln", [NH, 1], F32)
                g_negam = sb(ctx, "c_gnegam", [NH, 1], F32)
                g_tmpel = sb(ctx, "c_gtmpel", [NH, 1], F32)
                g_sct = sb(ctx, "c_gsct", [NH, 1], F32)
                g_elv = sb(ctx, "c_gelv", [NH, 1], F32)
                g_sc = sb(ctx, "c_gsc", [NH, 1], F32)
                g_dec = sb(ctx, "c_gdec", [NH, 1], F32)
                gates_tok = sb(ctx, "c_gtok", [128, 24], F32)
                bcrow = sb(ctx, "c_bcrow", [1, 24], F32)
                bc_sb = sb(ctx, "c_bcsb", [128, 24], F32)
                qe_tok = [sb(ctx, f"c_qet{i}", [128, 128], BF16) for i in range(2)]
                ke_tok = [sb(ctx, f"c_ket{i}", [128, 128], BF16) for i in range(2)]
                qe_fm = [sb(ctx, f"c_qef{i}", [128, 128], BF16) for i in range(2)]
                ke_fm = [sb(ctx, f"c_kef{i}", [128, 128], BF16) for i in range(2)]
                wbb = sb(ctx, "c_wbb", [128, 128], BF16)
                Cp = [sb(ctx, f"c_Cp{i}", [128, DV], BF16) for i in range(2)]
                np_bf = [sb(ctx, f"c_npbf{i}", [128, 1], BF16) for i in range(2)]
                hh = sb(ctx, "c_hh", [128, DV], BF16)
                Csc = sb(ctx, "c_Csc", [128, DV], F32)
                nscr = sb(ctx, "c_nscr", [128, 1], F32)
                qn_row = sb(ctx, "c_qnrow", [1, 128], F32)
                qn_tok = sb(ctx, "c_qntok", [128, NH], F32)
                aq8 = sb(ctx, "c_aq8", [128, NH], F32)
                den8 = sb(ctx, "c_den8", [128, NH], F32)
                dinv = sb(ctx, "c_dinv", [128, NH], F32)
                hcat = sb(ctx, "c_hcat", [128, D], BF16)
                hs_all = sb(ctx, "c_hsall", [128, D], F32)
                hob = sb(ctx, "c_hob", [128, D], BF16)
                sig = sb(ctx, "c_sig", [128, D], BF16)
                houtT = sb(ctx, "c_houtT", [128, D], BF16)
                y_sb = sb(ctx, "c_ysb", [128, D], F32)
                xp_sb = sb(ctx, "c_xpsb", [128, D], F32)
                xlnp_sb = sb(ctx, "c_xlnpsb", [128, D], BF16)
                jnky = sb(ctx, "c_jnky", [128, D], BF16)
                jnka = sb(ctx, "c_jnka", [128, DV], BF16)
                mh_sum = sb(ctx, "c_mhsum", [128, NH], F32)
                mh_sq = sb(ctx, "c_mhsq", [128, NH], F32)
                mean8 = sb(ctx, "c_mean8", [128, NH], F32)
                var8 = sb(ctx, "c_var8", [128, NH], F32)
                mm8 = sb(ctx, "c_mm8", [128, NH], F32)
                rstd8 = sb(ctx, "c_rstd8", [128, NH], F32)
                ysum = sb(ctx, "c_ysum", [128, 1], F32)
                ysq = sb(ctx, "c_ysq", [128, 1], F32)
                ymean = sb(ctx, "c_ymean", [128, 1], F32)
                yvar = sb(ctx, "c_yvar", [128, 1], F32)
                ymm = sb(ctx, "c_ymm", [128, 1], F32)
                yrstd = sb(ctx, "c_yrstd", [128, 1], F32)
                xsum = sb(ctx, "c_xsum", [128, 1], F32)
                xsq = sb(ctx, "c_xsq", [128, 1], F32)
                if dbg:
                    d_qn = sb(ctx, "d_qn", [128, NH], F32)
                    d_gt = sb(ctx, "d_gt", [128, 24], F32)
                    d_dinv = sb(ctx, "d_dinv", [128, NH], F32)
                    d_hcat = sb(ctx, "d_hcat", [128, D], BF16)
                    d_st = sb(ctx, "d_st", [128, 4 * NH], F32)
                    d_ysb = sb(ctx, "d_ysb", [128, D], F32)
                xmean = sb(ctx, "c_xmean", [128, 1], F32)
                xvar = sb(ctx, "c_xvar", [128, 1], F32)
                xmm = sb(ctx, "c_xmm", [128, 1], F32)
                xrstd = sb(ctx, "c_xrstd", [128, 1], F32)

                p_small = ctx.enter_context(nc.psum_tensor("c_psmall", [128, 512], F32))
                p_t2 = [ctx.enter_context(nc.psum_tensor(f"c_pt2{i}", [128, 256], BF16)) for i in range(2)]
                p_s2 = [ctx.enter_context(nc.psum_tensor(f"c_ps{i}", [128, 128], F32)) for i in range(2)]
                p_h = ctx.enter_context(nc.psum_tensor("c_ph", [128, DV], F32))
                p_kv = ctx.enter_context(nc.psum_tensor("c_pkv", [128, 512], F32))
                p_y = ctx.enter_context(nc.psum_tensor("c_py", [128, 512], F32))

                s_in = [Sem(octx, nc, f"c_sin{i}") for i in range(2)]
                s_ini = Sem(octx, nc, "c_sini")
                s_tw2 = Sem(octx, nc, "c_stw2")
                s_gmt = Sem(octx, nc, "c_sgmt")
                s_gme = Sem(octx, nc, "c_sgme")
                s_fold = Sem(octx, nc, "c_sfold")
                s_t1 = Sem(octx, nc, "c_st1")
                s_t2 = Sem(octx, nc, "c_st2")
                s_t3 = Sem(octx, nc, "c_st3")
                s_t4 = Sem(octx, nc, "c_st4")
                s_t5 = Sem(octx, nc, "c_st5")
                s_t6 = Sem(octx, nc, "c_st6")
                s_t7 = Sem(octx, nc, "c_st7")
                s_t8 = Sem(octx, nc, "c_st8")
                s_t9 = Sem(octx, nc, "c_st9")
                s_t10 = Sem(octx, nc, "c_st10")
                s_t11 = Sem(octx, nc, "c_st11")
                s_t12 = Sem(octx, nc, "c_st12")
                s_gv = Sem(octx, nc, "c_sgv")
                s_gs = Sem(octx, nc, "c_sgs")
                s_e2 = Sem(octx, nc, "c_se2")
                s_v3 = Sem(octx, nc, "c_sv3")
                s_qe = Sem(octx, nc, "c_sqe")
                s_fme = Sem(octx, nc, "c_sfme")
                s_wpp = Sem(octx, nc, "c_swpp")
                s_cpb = Sem(octx, nc, "c_scpb")
                s_npb = Sem(octx, nc, "c_snpb")
                s_hev = Sem(octx, nc, "c_shev")
                s_cup = Sem(octx, nc, "c_scup")
                s_qnr = Sem(octx, nc, "c_sqnr")
                s_hte = Sem(octx, nc, "c_shte")
                s_qnt = Sem(octx, nc, "c_sqnt")
                s_hsf = Sem(octx, nc, "c_shsf")
                s_mhs = Sem(octx, nc, "c_smhs")
                s_mvar = Sem(octx, nc, "c_smvar")
                s_mrstd = Sem(octx, nc, "c_smrstd")
                s_sig = Sem(octx, nc, "c_ssig")
                s_hout = Sem(octx, nc, "c_shout")
                s_houtev = Sem(octx, nc, "c_shoutev")
                s_yev = Sem(octx, nc, "c_syev")
                s_xp = Sem(octx, nc, "c_sxp")
                s_sts2 = Sem(octx, nc, "c_ssts2")
                s_var2S = Sem(octx, nc, "c_svar2")
                s_rstd2S = Sem(octx, nc, "c_srstd2")
                s_xlnp = Sem(octx, nc, "c_sxlnp")
                s_oxp = Sem(octx, nc, "c_soxp")
                s_oxl = Sem(octx, nc, "c_soxl")
                if dbg:
                    s_ohb = Sem(octx, nc, "c_sohb")
                    s_dbg = Sem(octx, nc, "c_sdbg")
                blk = ctx.enter_context(nc.Block())

                @blk.sync
                def _(sync):
                    s_ini.inc(sync.dma_start(out=sbig[:], in_=big), 16)
                    s_ini.inc(sync.dma_start(out=sbfg[:], in_=bfg), 16)
                    s_ini.inc(sync.dma_start(out=gf_sb[:], in_=gamma.rearrange("h (a b) -> (h a) b", b=128)), 16)
                    for hd in range(NH):
                        s_ini.inc(sync.dma_start(out=C_st[:, hd * DV:(hd + 1) * DV], in_=zero2), 16)
                    s_ini.inc(sync.dma_start(out=n_st[:], in_=zero2[:, 0:NH]), 16)
                    s_ini.inc(sync.dma_start(out=m_st[:], in_=zero2[0:NH, 0:1], allow_slow_non_contiguous=True), 16)
                NINI = s_ini.n

                @blk.scalar
                def _(scalar):
                    for kk in range(16):
                        s_tw2.inc(scalar.dma_start_transpose(
                            out=woutT[:, kk * D:(kk + 1) * D],
                            in_=woutbf[:, kk * 128:(kk + 1) * 128]), 16)
                    scalar.wait_ge(s_tw2.h, s_tw2.n)
                NTW2 = s_tw2.n

                @blk.tensor
                def _(tensor):
                    tensor.wait_ge(s_ini.h, NINI)
                    s_gmt.inc(tensor.transpose(p_small[:, 0:16], gf_sb[:], c_idf[0:16, 0:16]))

                @blk.vector
                def _(vector):
                    vector.wait_ge(s_gmt.h, 1)
                    s_gme.inc(vector.tensor_copy(gammaT[:], p_small[:, 0:16]))
                    vector.wait_ge(s_tw2.h, NTW2)
                    for kk in range(16):
                        vector.tensor_scalar(out=jnky[:], in0=woutT[:, kk * D:(kk + 1) * D],
                                             scalar1=gammaT[:, kk:kk + 1], scalar2=None, op0=A.mult)
                        fold_last = vector.tensor_copy(woutT[:, kk * D:(kk + 1) * D], jnky[:])
                    s_fold.inc(fold_last)

                @blk.sync
                def _(sync):
                    for c in range(NCH):
                        if c >= 2:
                            sync.wait_ge(s_xp.h, c - 1)
                            sync.wait_ge(s_sig.h, c - 1)
                        p = c % 2
                        sl = slice(c * L, (c + 1) * L)
                        s_in[p].inc(sync.dma_start(out=q_in[p][:], in_=qd[sl, :]), 16)
                        s_in[p].inc(sync.dma_start(out=k_in[p][:], in_=kd[sl, :]), 16)
                        s_in[p].inc(sync.dma_start(out=if_in[p][:], in_=ifd[sl, :]), 16)
                        s_in[p].inc(sync.dma_start(out=v_in[p][:], in_=vd[sl, :]), 16)
                        s_in[p].inc(sync.dma_start(out=og_in[p][:], in_=ogd[sl, :]), 16)
                        s_in[p].inc(sync.dma_start(out=x_in[p][:], in_=x[sl, :]), 16)

                @blk.tensor
                def _(tensor):
                    tensor.wait_ge(s_fold.h, 1)
                    for c in range(NCH):
                        p = c % 2
                        H8 = 8 * c
                        tensor.wait_ge(s_in[p].h, 96 * (c // 2 + 1))
                        tensor.wait_ge(s_qnt.h, c)
                        tensor.transpose(p_small[0:NH, 0:128], if_in[p][:, 0:NH], c_idf[:])
                        s_t1.inc(tensor.transpose(p_small[0:NH, 128:256], if_in[p][:, NH:2 * NH], c_idf[:]))
                        tensor.wait_ge(s_gs.h, 3 * c + 3)
                        tensor.wait_ge(s_e2.h, c)
                        tensor.transpose(p_small[:, 256:264], g_e1[:], c_idf[0:NH, 0:NH])
                        tensor.transpose(p_small[:, 264:272], g_ea[:], c_idf[0:NH, 0:NH])
                        s_t2.inc(tensor.transpose(p_small[:, 272:280], g_em[:], c_idf[0:NH, 0:NH]))
                        tensor.wait_ge(s_gv.h, 4 * c + 4)
                        tensor.transpose(p_small[0:1, 280:288], g_dec[:], c_idf[0:NH, 0:NH])
                        tensor.transpose(p_small[0:1, 288:296], g_sc[:], c_idf[0:NH, 0:NH])
                        s_t2.inc(tensor.transpose(p_small[0:1, 296:304], g_elv[:], c_idf[0:NH, 0:NH]))
                        tensor.wait_ge(s_e2.h, c + 1)
                        tensor.wait_ge(s_v3.h, c)
                        s_t3.inc(tensor.matmul(p_small[:, 304:328], c_onesf[0:1, 0:128],
                                               bcrow[:], start=True, stop=True))
                        def T4(hd):
                            hb = hd % 2
                            tensor.wait_ge(s_qe.h, H8 + hd + 1)
                            if H8 + hd - 1 >= 1:
                                tensor.wait_ge(s_hte.h, H8 + hd - 1)
                            tensor.transpose(p_t2[hb][:, 0:128], qe_tok[hb][:], c_idb[:])
                            s_t4.inc(tensor.transpose(p_t2[hb][:, 128:256], ke_tok[hb][:], c_idb[:]))
                        T4(0)
                        for hd in range(NH):
                            hb = hd % 2
                            if hd + 1 < NH:
                                T4(hd + 1)
                            tensor.wait_ge(s_fme.h, H8 + hd + 1)
                            if H8 + hd - 1 >= 1:
                                tensor.wait_ge(s_wpp.h, H8 + hd - 1)
                            s_t5.inc(tensor.matmul(p_s2[hb][:], ke_fm[hb][:], qe_fm[hb][:],
                                                   start=True, stop=True))
                            tensor.wait_ge(s_cup.h, H8 + hd)
                            tensor.matmul(p_kv[:, 0:DV], ke_tok[hb][:], v_in[p][:, hd * DV:(hd + 1) * DV],
                                          start=True, stop=True)
                            s_t7.inc(tensor.matmul(p_kv[:, DV:DV + 1], ke_tok[hb][:], c_ones[:, 0:1],
                                                   start=True, stop=True))
                            tensor.wait_ge(s_wpp.h, H8 + hd + 1)
                            tensor.wait_ge(s_cpb.h, H8 + hd + 1)
                            tensor.wait_ge(s_hev.h, H8 + hd)
                            tensor.matmul(p_h[:, 0:128], Cp[hb][:, 0:128], qe_fm[hb][:], start=True, stop=False)
                            tensor.matmul(p_h[:, 0:128], v_in[p][:, hd * DV:hd * DV + 128], wbb[:],
                                          start=False, stop=True)
                            tensor.matmul(p_h[:, 128:256], Cp[hb][:, 128:256], qe_fm[hb][:], start=True, stop=False)
                            s_t6.inc(tensor.matmul(p_h[:, 128:256], v_in[p][:, hd * DV + 128:(hd + 1) * DV],
                                                   wbb[:], start=False, stop=True))
                            tensor.wait_ge(s_npb.h, H8 + hd + 1)
                            tensor.wait_ge(s_qnr.h, H8 + hd)
                            tensor.matmul(p_small[0:1, 340:468], c_ones[:, 0:1], wbb[:],
                                          start=True, stop=False)
                            s_t8.inc(tensor.matmul(p_small[0:1, 340:468], np_bf[hb][:], qe_fm[hb][:],
                                                   start=False, stop=True))
                            tensor.wait_ge(s_qnr.h, H8 + hd + 1)
                            s_t9.inc(tensor.transpose(p_small[:, 332 + hd:333 + hd], qn_row[:],
                                                      c_idf[0:1, 0:1]))
                            tensor.wait_ge(s_hev.h, H8 + hd + 1)
                            tensor.transpose(p_t2[hb][:, 0:128], hh[:, 0:128], c_idb[:])
                            s_t10.inc(tensor.transpose(p_t2[hb][:, 128:256], hh[:, 128:256], c_idb[:]))
                        tensor.wait_ge(s_hout.h, c + 1)
                        tensor.wait_ge(s_hte.h, 8 * (c + 1))
                        for kk in range(16):
                            if 16 * c + kk - 1 >= 1:
                                tensor.wait_ge(s_houtev.h, 16 * c + kk - 1)
                            dst = p_t2[kk % 2]
                            s_t11.inc(tensor.transpose(dst[:, 0:128], hob[:, kk * 128:(kk + 1) * 128], c_idb[:]))
                        for j in range(4):
                            tensor.wait_ge(s_yev.h, 4 * c + j)
                            if j == 0:
                                pass
                            else:
                                tensor.wait_ge(s_houtev.h, 16 * (c + 1))
                            for kk in range(16):
                                if j == 0:
                                    tensor.wait_ge(s_houtev.h, 16 * c + kk + 1)
                                lhsT = houtT[:, kk * 128:(kk + 1) * 128]
                                mml = tensor.matmul(p_y[:], lhsT,
                                                    woutT[:, kk * D + j * 512:kk * D + (j + 1) * 512],
                                                    start=(kk == 0), stop=(kk == 15))
                            s_t12.inc(mml)

                @blk.vector
                def _(vector):
                    vector.wait_ge(s_ini.h, NINI)
                    for c in range(NCH):
                        p = c % 2
                        H8 = 8 * c
                        vector.wait_ge(s_t1.h, c + 1)
                        vector.tensor_copy(g_i1[:], p_small[0:NH, 0:128])
                        vector.tensor_copy(g_f1[:], p_small[0:NH, 128:256])
                        vector.tensor_scalar(out=g_i1[:], in0=g_i1[:], scalar1=sbig[:], scalar2=None, op0=A.add)
                        s_gv.inc(vector.tensor_scalar(out=g_f1[:], in0=g_f1[:], scalar1=sbfg[:],
                                                      scalar2=None, op0=A.add))
                        vector.wait_ge(s_gs.h, 3 * c + 1)
                        vector.tensor_scalar(out=g_ic[:], in0=g_ti[:], scalar1=CAP, scalar2=None, op0=A.mult)
                        vector.tensor_scalar(out=g_fc[:], in0=g_tf[:], scalar1=CAP, scalar2=None, op0=A.mult)
                        vg(vector, vector.tensor_scalar(out=g_va[:], in0=g_fc[:], scalar1=-1.0,
                                                        scalar2=None, op0=A.mult))
                        s_gv.inc(vector.tensor_tensor(out=g_va[:], in0=g_va[:], in1=g_fc[:], op=A.max))
                        vector.wait_ge(s_gs.h, 3 * c + 2)
                        vector.tensor_scalar(out=g_mn[:], in0=g_fc[:], scalar1=0.0, scalar2=None, op0=A.min)
                        vg(vector, vector.tensor_tensor(out=g_fl[:], in0=g_mn[:], in1=g_l[:], op=A.subtract))
                        vg(vector, vector.tensor_tensor_scan(out=g_b[:], data0=g_fl[:], data1=c_zero[:],
                                                             initial=0.0, op0=A.add, op1=A.add))
                        vg(vector, vector.tensor_tensor_scan(out=g_m[:], data0=g_fl[:], data1=g_ic[:],
                                                             initial=m_st[:], op0=A.add, op1=A.max))
                        vg(vector, vector.tensor_tensor(out=g_a[:], in0=g_ic[:], in1=g_b[:], op=A.subtract))
                        vg(vector, vector.reduce_max(out=g_amax[:], in_=g_a[:], axis=mybir.AxisListType.X))
                        vg(vector, vector.tensor_tensor(out=g_t1[:], in0=g_b[:], in1=g_m[:], op=A.subtract))
                        vector.tensor_scalar(out=g_amln[:], in0=g_amax[:], scalar1=LNQK, scalar2=None, op0=A.add)
                        vector.tensor_scalar(out=g_negam[:], in0=g_amax[:], scalar1=-1.0, scalar2=None, op0=A.mult)
                        vg(vector, vector.tensor_tensor(out=g_tmpel[:], in0=g_t1[:, L - 1:L], in1=g_amax[:], op=A.add))
                        vg(vector, vector.tensor_tensor(out=g_sct[:], in0=m_st[:], in1=g_amax[:], op=A.subtract))
                        s_gv.inc(vector.tensor_copy(m_st[:], g_m[:, L - 1:L]))
                        vector.wait_ge(s_gs.h, 3 * c + 3)
                        s_gv.inc(vector.tensor_tensor(out=g_dec[:], in0=g_sc[:], in1=g_elv[:], op=A.mult))
                        vector.wait_ge(s_gv.h, s_gv.n)
                        vector.wait_ge(s_t2.h, 2 * c + 2)
                        vector.tensor_copy(gates_tok[:], p_small[:, 256:280])
                        s_e2.inc(vector.tensor_copy(bcrow[:], p_small[0:1, 280:304]))
                        vector.wait_ge(s_t3.h, c + 1)
                        s_v3.inc(vector.tensor_copy(bc_sb[:], p_small[:, 304:328]))
                        def VPREP(hd):
                            hb = hd % 2
                            if H8 + hd - 1 >= 1:
                                vector.wait_ge(s_t8.h, H8 + hd - 1)
                            s_cpb.inc(vector.tensor_scalar(out=Cp[hb][:], in0=C_st[:, hd * DV:(hd + 1) * DV],
                                                           scalar1=bc_sb[:, 8 + hd:9 + hd], scalar2=None,
                                                           op0=A.mult))
                            s_npb.inc(vector.tensor_scalar(out=np_bf[hb][:], in0=n_st[:, hd:hd + 1],
                                                           scalar1=bc_sb[:, 8 + hd:9 + hd], scalar2=None,
                                                           op0=A.mult))
                            vector.tensor_scalar(out=qe_tok[hb][:], in0=q_in[p][:, hd * 128:(hd + 1) * 128],
                                                 scalar1=gates_tok[:, hd:hd + 1], scalar2=None, op0=A.mult)
                            s_qe.inc(vector.tensor_scalar(out=ke_tok[hb][:],
                                                          in0=k_in[p][:, hd * 128:(hd + 1) * 128],
                                                          scalar1=gates_tok[:, 8 + hd:9 + hd], scalar2=None,
                                                          op0=A.mult))
                        VPREP(0)
                        for hd in range(NH):
                            hb = hd % 2
                            vector.wait_ge(s_t4.h, H8 + hd + 1)
                            vector.tensor_copy(qe_fm[hb][:], p_t2[hb][:, 0:128])
                            s_fme.inc(vector.tensor_copy(ke_fm[hb][:], p_t2[hb][:, 128:256]))
                            if hd + 1 < NH:
                                VPREP(hd + 1)
                            vector.wait_ge(s_t5.h, H8 + hd + 1)
                            s_wpp.inc(vector.tensor_tensor(out=wbb[:], in0=p_s2[hb][:], in1=c_mask[:], op=A.mult))
                            vector.wait_ge(s_t7.h, H8 + hd + 1)
                            vg(vector, vector.tensor_scalar(out=Csc[:], in0=C_st[:, hd * DV:(hd + 1) * DV],
                                                            scalar1=bc_sb[:, hd:hd + 1], scalar2=None, op0=A.mult))
                            vector.scalar_tensor_tensor(out=C_st[:, hd * DV:(hd + 1) * DV], in0=p_kv[:, 0:DV],
                                                        scalar=bc_sb[:, 16 + hd:17 + hd], in1=Csc[:],
                                                        op0=A.mult, op1=A.add)
                            vg(vector, vector.tensor_scalar(out=nscr[:], in0=n_st[:, hd:hd + 1],
                                                            scalar1=bc_sb[:, hd:hd + 1], scalar2=None, op0=A.mult))
                            s_cup.inc(vector.scalar_tensor_tensor(out=n_st[:, hd:hd + 1], in0=p_kv[:, DV:DV + 1],
                                                                  scalar=bc_sb[:, 16 + hd:17 + hd], in1=nscr[:],
                                                                  op0=A.mult, op1=A.add))
                            vector.wait_ge(s_t6.h, H8 + hd + 1)
                            s_hev.inc(vector.tensor_copy(hh[:], p_h[:]))
                            vector.wait_ge(s_t8.h, H8 + hd + 1)
                            s_qnr.inc(vector.tensor_copy(qn_row[:], p_small[0:1, 340:468]))
                            vector.wait_ge(s_t10.h, H8 + hd + 1)
                            vector.tensor_copy(hcat[:, hd * DV:hd * DV + 128], p_t2[hb][:, 0:128])
                            s_hte.inc(vector.tensor_copy(hcat[:, hd * DV + 128:(hd + 1) * DV],
                                                         p_t2[hb][:, 128:256]))
                        vector.wait_ge(s_t9.h, 8 * (c + 1))
                        s_qnt.inc(vector.tensor_copy(qn_tok[:], p_small[:, 332:340]))
                        if dbg and c == 0:
                            vector.tensor_copy(d_hcat[:], hcat[:])
                            vector.tensor_copy(d_qn[:], qn_tok[:])
                            s_dbg.inc(vector.tensor_copy(d_gt[:], gates_tok[:]))
                        vector.wait_ge(s_qnt.h, s_qnt.n)
                        vg(vector, vector.tensor_scalar(out=aq8[:], in0=qn_tok[:], scalar1=-1.0,
                                                        scalar2=None, op0=A.mult))
                        vg(vector, vector.tensor_tensor(out=aq8[:], in0=aq8[:], in1=qn_tok[:], op=A.max))
                        vg(vector, vector.tensor_tensor(out=den8[:], in0=aq8[:], in1=gates_tok[:, 16:24], op=A.max))
                        vg(vector, vector.tensor_scalar(out=den8[:], in0=den8[:], scalar1=1e-6, scalar2=None, op0=A.add))
                        vg(vector, vector.reciprocal(out=dinv[:], in_=den8[:]))
                        if dbg and c == 0:
                            s_dbg.inc(vector.tensor_copy(d_dinv[:], dinv[:]))
                        if c >= 1:
                            vector.wait_ge(s_mhs.h, c)
                        for hd in range(NH):
                            s_hsf.inc(vector.tensor_scalar(out=hs_all[:, hd * DV:(hd + 1) * DV],
                                                           in0=hcat[:, hd * DV:(hd + 1) * DV],
                                                           scalar1=dinv[:, hd:hd + 1], scalar2=None, op0=A.mult))
                        vector.wait_ge(s_mhs.h, c + 1)
                        vector.tensor_scalar(out=mean8[:], in0=mh_sum[:], scalar1=1.0 / DV, scalar2=None, op0=A.mult)
                        vg(vector, vector.tensor_scalar(out=var8[:], in0=mh_sq[:], scalar1=1.0 / DV, scalar2=None, op0=A.mult))
                        vg(vector, vector.tensor_tensor(out=mm8[:], in0=mean8[:], in1=mean8[:], op=A.mult))
                        vg(vector, vector.tensor_tensor(out=var8[:], in0=var8[:], in1=mm8[:], op=A.subtract))
                        s_mvar.inc(vector.tensor_scalar(out=var8[:], in0=var8[:], scalar1=EPS_MH,
                                                        scalar2=None, op0=A.add))
                        vector.wait_ge(s_mrstd.h, c + 1)
                        if dbg and c == 0:
                            vector.tensor_copy(d_st[:, 0:NH], mean8[:])
                            vector.tensor_copy(d_st[:, NH:2 * NH], var8[:])
                            vector.tensor_copy(d_st[:, 2 * NH:3 * NH], rstd8[:])
                            s_dbg.inc(vector.tensor_copy(d_st[:, 3 * NH:4 * NH], mh_sq[:]))
                        for hd in range(NH):
                            vector.tensor_scalar(out=hcat[:, hd * DV:(hd + 1) * DV],
                                                 in0=hs_all[:, hd * DV:(hd + 1) * DV],
                                                 scalar1=mean8[:, hd:hd + 1], scalar2=rstd8[:, hd:hd + 1],
                                                 op0=A.subtract, op1=A.mult)
                        vector.wait_ge(s_sig.h, c + 1)
                        if dbg and c >= 1:
                            vector.wait_ge(s_ohb.h, 16 * c)
                        s_hout.inc(vector.tensor_tensor(out=hob[:], in0=hcat[:], in1=sig[:], op=A.mult))
                        for kk in range(16):
                            vector.wait_ge(s_t11.h, 16 * c + kk + 1)
                            src = p_t2[kk % 2]
                            s_houtev.inc(vector.tensor_copy(houtT[:, kk * 128:(kk + 1) * 128], src[:, 0:128]))
                        if c >= 1:
                            vector.wait_ge(s_sts2.h, 2 * c)
                        for j in range(4):
                            vector.wait_ge(s_t12.h, 4 * c + j + 1)
                            s_yev.inc(vector.tensor_copy(y_sb[:, j * 512:(j + 1) * 512], p_y[:]))
                        if dbg and c == 0:
                            s_dbg.inc(vector.tensor_copy(d_ysb[:], y_sb[:]))
                        vector.wait_ge(s_sts2.h, 2 * c + 1)
                        vector.tensor_scalar(out=ymean[:], in0=ysum[:], scalar1=1.0 / D, scalar2=None, op0=A.mult)
                        vg(vector, vector.tensor_scalar(out=yvar[:], in0=ysq[:], scalar1=1.0 / D, scalar2=None, op0=A.mult))
                        vg(vector, vector.tensor_tensor(out=ymm[:], in0=ymean[:], in1=ymean[:], op=A.mult))
                        vg(vector, vector.tensor_tensor(out=yvar[:], in0=yvar[:], in1=ymm[:], op=A.subtract))
                        s_var2S.inc(vector.tensor_scalar(out=yvar[:], in0=yvar[:], scalar1=EPS_LN,
                                                         scalar2=None, op0=A.add))
                        vector.wait_ge(s_rstd2S.h, 2 * c + 1)
                        if c >= 1:
                            vector.wait_ge(s_oxp.h, 16 * c)
                        vg(vector, vector.tensor_scalar(out=hs_all[:], in0=y_sb[:], scalar1=ymean[:],
                                                        scalar2=yrstd[:], op0=A.subtract, op1=A.mult))
                        s_xp.inc(vector.tensor_tensor(out=xp_sb[:], in0=hs_all[:], in1=x_in[p][:], op=A.add))
                        vector.wait_ge(s_xp.h, s_xp.n)
                        vector.wait_ge(s_sts2.h, 2 * c + 2)
                        vector.tensor_scalar(out=xmean[:], in0=xsum[:], scalar1=1.0 / D, scalar2=None, op0=A.mult)
                        vg(vector, vector.tensor_scalar(out=xvar[:], in0=xsq[:], scalar1=1.0 / D, scalar2=None, op0=A.mult))
                        vg(vector, vector.tensor_tensor(out=xmm[:], in0=xmean[:], in1=xmean[:], op=A.mult))
                        vg(vector, vector.tensor_tensor(out=xvar[:], in0=xvar[:], in1=xmm[:], op=A.subtract))
                        s_var2S.inc(vector.tensor_scalar(out=xvar[:], in0=xvar[:], scalar1=EPS_LN,
                                                         scalar2=None, op0=A.add))
                        vector.wait_ge(s_rstd2S.h, 2 * c + 2)
                        if c >= 1:
                            vector.wait_ge(s_oxl.h, 16 * c)
                        s_xlnp.inc(vector.tensor_scalar(
                            out=xlnp_sb[:].rearrange("p (s u) -> p s u", u=512),
                            in0=xp_sb[:].rearrange("p (u s) -> p s u", s=4),
                            scalar1=xmean[:], scalar2=xrstd[:], op0=A.subtract, op1=A.mult))

                @blk.scalar
                def _(scalar):
                    for c in range(NCH):
                        p = c % 2
                        H8 = 8 * c
                        scalar.wait_ge(s_gv.h, 4 * c + 1)
                        scalar.activation(out=g_ti[:], in_=g_i1[:], func=AF.Tanh, scale=1.0 / CAP)
                        s_gs.inc(scalar.activation(out=g_tf[:], in_=g_f1[:], func=AF.Tanh, scale=1.0 / CAP))
                        scalar.wait_ge(s_gv.h, 4 * c + 2)
                        scalar.activation(out=g_e[:], in_=g_va[:], func=AF.Exp, scale=-1.0)
                        s_gs.inc(scalar.activation(out=g_l[:], in_=g_e[:], func=AF.Ln, bias=1.0))
                        scalar.wait_ge(s_gv.h, 4 * c + 3)
                        scalar.activation(out=g_e1[:], in_=g_t1[:], func=AF.Exp, bias=g_amln[:])
                        scalar.activation(out=g_ea[:], in_=g_a[:], func=AF.Exp, bias=g_negam[:])
                        scalar.activation(out=g_em[:], in_=g_m[:], func=AF.Exp, scale=-1.0)
                        scalar.activation(out=g_elv[:], in_=g_tmpel[:], func=AF.Exp)
                        s_gs.inc(scalar.activation(out=g_sc[:], in_=g_sct[:], func=AF.Exp))
                        scalar.wait_ge(s_in[p].h, 96 * (c // 2 + 1))
                        if c >= 1:
                            scalar.wait_ge(s_hout.h, c)
                        s_sig.inc(scalar.activation(out=sig[:], in_=og_in[p][:], func=AF.Sigmoid))
                        for hd in range(NH):
                            scalar.wait_ge(s_hsf.h, H8 + hd + 1)
                            scalar.activation(out=jnka[:], in_=hs_all[:, hd * DV:(hd + 1) * DV],
                                              func=AF.Square, accum_out=mh_sq[:, hd:hd + 1])
                            mha = scalar.activation(out=jnka[:], in_=hs_all[:, hd * DV:(hd + 1) * DV],
                                                    func=AF.Identity, accum_out=mh_sum[:, hd:hd + 1])
                        s_mhs.inc(mha)
                        scalar.wait_ge(s_mvar.h, c + 1)
                        scalar.activation(out=rstd8[:], in_=var8[:], func=AF.Ln)
                        s_mrstd.inc(scalar.activation(out=rstd8[:], in_=rstd8[:], func=AF.Exp, scale=-0.5))
                        scalar.wait_ge(s_yev.h, 4 * c + 4)
                        scalar.activation(out=jnky[:], in_=y_sb[:], func=AF.Square, accum_out=ysq[:])
                        s_sts2.inc(scalar.activation(out=jnky[:], in_=y_sb[:], func=AF.Identity,
                                                     accum_out=ysum[:]))
                        scalar.wait_ge(s_var2S.h, 2 * c + 1)
                        scalar.activation(out=yrstd[:], in_=yvar[:], func=AF.Ln)
                        s_rstd2S.inc(scalar.activation(out=yrstd[:], in_=yrstd[:], func=AF.Exp, scale=-0.5))
                        scalar.wait_ge(s_xp.h, c + 1)
                        scalar.activation(out=jnky[:], in_=xp_sb[:], func=AF.Square, accum_out=xsq[:])
                        s_sts2.inc(scalar.activation(out=jnky[:], in_=xp_sb[:], func=AF.Identity,
                                                     accum_out=xsum[:]))
                        scalar.wait_ge(s_var2S.h, 2 * c + 2)
                        scalar.activation(out=xrstd[:], in_=xvar[:], func=AF.Ln)
                        s_rstd2S.inc(scalar.activation(out=xrstd[:], in_=xrstd[:], func=AF.Exp, scale=-0.5))

                @blk.gpsimd
                def _(g):
                    for c in range(NCH):
                        sl = slice(c * L, (c + 1) * L)
                        g.wait_ge(s_xp.h, c + 1)
                        s_oxp.inc(g.dma_start(out=xpd[sl, :], in_=xp_sb[:]), 16)
                        g.wait_ge(s_xlnp.h, c + 1)
                        s_oxl.inc(g.dma_start(out=xlnpd[sl, :], in_=xlnp_sb[:]), 16)
                        if dbg:
                            g.wait_ge(s_hout.h, c + 1)
                            s_ohb.inc(g.dma_start(out=hobd[sl, :], in_=hob[:]), 16)
                    g.wait_ge(s_oxp.h, s_oxp.n)
                    g.wait_ge(s_oxl.h, s_oxl.n)
                    if dbg:
                        g.wait_ge(s_ohb.h, s_ohb.n)
                        g.wait_ge(s_dbg.h, s_dbg.n)
                        s_ohb.inc(g.dma_start(out=dbg_qn, in_=d_qn[:]), 16)
                        s_ohb.inc(g.dma_start(out=dbg_gt, in_=d_gt[:]), 16)
                        s_ohb.inc(g.dma_start(out=dbg_dinv, in_=d_dinv[:]), 16)
                        s_ohb.inc(g.dma_start(out=dbg_hcat, in_=d_hcat[:]), 16)
                        s_ohb.inc(g.dma_start(out=dbg_stats, in_=d_st[:]), 16)
                        s_ohb.inc(g.dma_start(out=dbg_ysb, in_=d_ysb[:]), 16)
                        g.wait_ge(s_ohb.h, s_ohb.n)

        if "p4" in phases:
            with ExitStack() as ctx:
                xq4 = sb(ctx, "f_xq4", [128, 16 * 1024], BF16)
                win_g = [sb(ctx, f"f_wg{i}", [128, 16 * 512], BF16) for i in range(2)]
                win_u = [sb(ctx, f"f_wu{i}", [128, 16 * 512], BF16) for i in range(2)]
                sg0 = sb(ctx, "f_sg0", [128, 512], BF16)
                sg1 = sb(ctx, "f_sg1", [128, 512], BF16)
                Hst = [sb(ctx, f"f_H{i}", [128, 1024], BF16) for i in range(2)]
                p_g = [ctx.enter_context(nc.psum_tensor(f"f_pg{i}", [128, 1024], F32)) for i in range(2)]
                p_u2 = [ctx.enter_context(nc.psum_tensor(f"f_pu{i}", [128, 1024], F32)) for i in range(2)]
                s_xq = Sem(octx, nc, "f_sxq")
                s_w = [Sem(octx, nc, f"f_sw{i}") for i in range(2)]
                s_mm = Sem(octx, nc, "f_smm")
                s_sil = Sem(octx, nc, "f_ssil")
                s_hh = Sem(octx, nc, "f_shh")
                s_ho = Sem(octx, nc, "f_sho")
                blk = ctx.enter_context(nc.Block())

                @blk.sync
                def _(sync):
                    for q4 in range(4):
                        for ub in range(11):
                            gi = q4 * 11 + ub
                            pp = gi % 2
                            if gi >= 2:
                                sync.wait_ge(s_mm.h, 4 * (gi - 1))
                            for kk in range(16):
                                s_w[pp].inc(sync.dma_start(
                                    out=win_g[pp][:, kk * 512:(kk + 1) * 512],
                                    in_=wguT_d[128 * kk:128 * (kk + 1), 512 * ub:512 * (ub + 1)]), 16)
                                s_w[pp].inc(sync.dma_start(
                                    out=win_u[pp][:, kk * 512:(kk + 1) * 512],
                                    in_=wguT_d[128 * kk:128 * (kk + 1),
                                               I_FFN + 512 * ub:I_FFN + 512 * (ub + 1)]), 16)

                @blk.tensor
                def _(tensor):
                    for q4 in range(4):
                        tensor.wait_ge(s_xq.h, 256 * (q4 + 1))
                        for ub in range(11):
                            gi = q4 * 11 + ub
                            pp = gi % 2
                            tensor.wait_ge(s_w[pp].h, 512 * (gi // 2 + 1))
                            for sp in range(4):
                                grp = 4 * gi + sp
                                pq = grp % 2
                                tensor.wait_ge(s_hh.h, max(0, grp - 1))
                                for kk in range(16):
                                    lg = win_g[pp][:, 512 * kk + sp:512 * (kk + 1):4]
                                    lu = win_u[pp][:, 512 * kk + sp:512 * (kk + 1):4]
                                    st = (kk == 0)
                                    sp_ = (kk == 15)
                                    tensor.matmul(p_g[pq][:, 0:512], lg, xq4[:, kk * 1024:kk * 1024 + 512],
                                                  start=st, stop=sp_)
                                    tensor.matmul(p_g[pq][:, 512:1024], lg,
                                                  xq4[:, kk * 1024 + 512:(kk + 1) * 1024], start=st, stop=sp_)
                                    tensor.matmul(p_u2[pq][:, 0:512], lu, xq4[:, kk * 1024:kk * 1024 + 512],
                                                  start=st, stop=sp_)
                                    mml = tensor.matmul(p_u2[pq][:, 512:1024], lu,
                                                        xq4[:, kk * 1024 + 512:(kk + 1) * 1024],
                                                        start=st, stop=sp_)
                                s_mm.inc(mml)

                @blk.scalar
                def _(scalar):
                    for q4 in range(4):
                        if q4 >= 1:
                            scalar.wait_ge(s_mm.h, 44 * q4)
                        for kk in range(16):
                            s_xq.inc(scalar.dma_start_transpose(
                                out=xq4[:, kk * 1024:(kk + 1) * 1024],
                                in_=xlnpd[q4 * 1024:(q4 + 1) * 1024, kk * 128:(kk + 1) * 128]), 16)
                        for grp in range(44 * q4, 44 * (q4 + 1)):
                            pq = grp % 2
                            scalar.wait_ge(s_mm.h, grp + 1)
                            if grp >= 1:
                                scalar.wait_ge(s_hh.h, grp)
                            scalar.activation(out=sg0[:], in_=p_g[pq][:, 0:512], func=AF.Silu)
                            s_sil.inc(scalar.activation(out=sg1[:], in_=p_g[pq][:, 512:1024], func=AF.Silu))

                @blk.vector
                def _(vector):
                    for grp in range(176):
                        pq = grp % 2
                        vector.wait_ge(s_sil.h, grp + 1)
                        if grp >= 2:
                            vector.wait_ge(s_ho.h, 16 * (grp - 1))
                        vector.tensor_tensor(out=Hst[pq][:, 0:512], in0=sg0[:], in1=p_u2[pq][:, 0:512],
                                             op=A.mult)
                        s_hh.inc(vector.tensor_tensor(out=Hst[pq][:, 512:1024], in0=sg1[:],
                                                      in1=p_u2[pq][:, 512:1024], op=A.mult))

                @blk.gpsimd
                def _(g):
                    for grp in range(176):
                        pq = grp % 2
                        q4, rem = divmod(grp, 44)
                        ub, sp = divmod(rem, 4)
                        g.wait_ge(s_hh.h, grp + 1)
                        row0 = 1408 * sp + 128 * ub
                        s_ho.inc(g.dma_start(out=H_d[row0:row0 + 128, 1024 * q4:1024 * (q4 + 1)],
                                             in_=Hst[pq][:]), 16)
                    g.wait_ge(s_ho.h, s_ho.n)

        if "p5" in phases:
            with ExitStack() as ctx:
                wdn = sb(ctx, "e_wdn", [128, 44 * 1024], BF16)
                Hgrp = [sb(ctx, f"e_H{i}", [128, 44 * 256], BF16) for i in range(2)]
                ysb = [sb(ctx, f"e_y{i}", [128, 1024], F32) for i in range(2)]
                y1t = sb(ctx, "e_y1t", [128, 1024], F32)
                xpt = sb(ctx, "e_xpt", [128, D], F32)
                ot = sb(ctx, "e_ot", [128, D], F32)
                jnk5 = sb(ctx, "e_jnkX", [128, 1024], BF16)
                s1a = sb(ctx, "e_s1a", [128, 1], F32)
                s1b = sb(ctx, "e_s1b", [128, 1], F32)
                s2a = sb(ctx, "e_s2a", [128, 1], F32)
                s2b = sb(ctx, "e_s2b", [128, 1], F32)
                ss1 = sb(ctx, "e_ss1", [128, 1], F32)
                ss2 = sb(ctx, "e_ss2", [128, 1], F32)
                q2m = sb(ctx, "e_q2m", [128, 1], F32)
                rinv = sb(ctx, "e_rinv", [128, 1], F32)
                rr2 = sb(ctx, "e_rr2", [128, 1], F32)
                mean2 = sb(ctx, "e_mean2", [128, 1], F32)
                var2 = sb(ctx, "e_var2", [128, 1], F32)
                m22 = sb(ctx, "e_m22", [128, 1], F32)
                rstd2 = sb(ctx, "e_rstd2", [128, 1], F32)
                alal = sb(ctx, "e_al", [128, 1], F32)
                bebe = sb(ctx, "e_be", [128, 1], F32)
                if dbg:
                    d_t5 = sb(ctx, "e_dt5", [128, 16], F32)
                    s_dt5 = Sem(octx, nc, "e_sdt5")
                    s_dt5s = Sem(octx, nc, "e_sdt5s")
                p_y5 = [ctx.enter_context(nc.psum_tensor(f"e_py{i}", [128, 1024], F32)) for i in range(2)]
                s_wd = Sem(octx, nc, "e_swd")
                s_Hg = [Sem(octx, nc, f"e_sH{i}") for i in range(2)]
                s_mm5 = Sem(octx, nc, "e_smm")
                s_ye = Sem(octx, nc, "e_sye")
                s_y1s = Sem(octx, nc, "e_sy1s")
                s_ld1 = Sem(octx, nc, "e_sld1")
                s_ac = Sem(octx, nc, "e_sac")
                s_vs = Sem(octx, nc, "e_svs")
                s_ri = Sem(octx, nc, "e_sri")
                s_vv = Sem(octx, nc, "e_svv")
                s_rs = Sem(octx, nc, "e_srs")
                s_oo = Sem(octx, nc, "e_soo")
                blk = ctx.enter_context(nc.Block())

                @blk.sync
                def _(sync):
                    for oh in range(2):
                        if oh == 1:
                            sync.wait_ge(s_mm5.h, 32)
                        for k in range(44):
                            s_wd.inc(sync.dma_start(
                                out=wdn[:, k * 1024:(k + 1) * 1024],
                                in_=wdnT_d[128 * k:128 * (k + 1), 1024 * oh:1024 * (oh + 1)]), 16)
                        for tg in range(16):
                            gj = oh * 16 + tg
                            pp = tg % 2
                            if gj >= 2:
                                sync.wait_ge(s_mm5.h, 2 * (gj - 1))
                            for k in range(44):
                                s_Hg[pp].inc(sync.dma_start(
                                    out=Hgrp[pp][:, k * 256:(k + 1) * 256],
                                    in_=H_d[128 * k:128 * (k + 1), 256 * tg:256 * (tg + 1)]), 16)
                            if oh == 1:
                                for ts2 in range(2):
                                    tsub = 2 * tg + ts2
                                    sl = slice(tsub * 128, (tsub + 1) * 128)
                                    if tsub >= 1:
                                        sync.wait_ge(s_oo.h, 16 * tsub)
                                    s_ld1.inc(sync.dma_start(out=y1t[:], in_=y1_d[sl, :]), 16)
                                    s_ld1.inc(sync.dma_start(out=xpt[:], in_=xpd[sl, :]), 16)
                                    sync.wait_ge(s_ld1.h, s_ld1.n)

                @blk.tensor
                def _(tensor):
                    for oh in range(2):
                        tensor.wait_ge(s_wd.h, 704 * (oh + 1))
                        for tg in range(16):
                            gj = oh * 16 + tg
                            pp = tg % 2
                            tensor.wait_ge(s_Hg[pp].h, 704 * (gj // 2 + 1))
                            for ts2 in range(2):
                                tsub = 2 * gj + ts2
                                pq = tsub % 2
                                tensor.wait_ge(s_ye.h, max(0, tsub - 1))
                                for k in range(44):
                                    lh = Hgrp[pp][:, 256 * k + 128 * ts2:256 * k + 128 * (ts2 + 1)]
                                    st = (k == 0)
                                    sp_ = (k == 43)
                                    tensor.matmul(p_y5[pq][:, 0:512], lh, wdn[:, k * 1024:k * 1024 + 512],
                                                  start=st, stop=sp_)
                                    mml = tensor.matmul(p_y5[pq][:, 512:1024], lh,
                                                        wdn[:, k * 1024 + 512:(k + 1) * 1024],
                                                        start=st, stop=sp_)
                                s_mm5.inc(mml)


                @blk.vector
                def _(vector):
                    for tsub in range(32):
                        pq = tsub % 2
                        vector.wait_ge(s_mm5.h, tsub + 1)
                        if tsub >= 2:
                            vector.wait_ge(s_y1s.h, 16 * (tsub - 1))
                        s_ye.inc(vector.tensor_copy(ysb[pq][:], p_y5[pq][:]))
                    for tsub in range(32):
                        pq = tsub % 2
                        vector.wait_ge(s_mm5.h, 33 + tsub)
                        if tsub < 2:
                            vector.wait_ge(s_y1s.h, 16 * 32)
                        else:
                            vector.wait_ge(s_ac.h, tsub - 1)
                        s_ye.inc(vector.tensor_copy(ysb[pq][:], p_y5[pq][:]))
                        vector.wait_ge(s_ac.h, tsub + 1)
                        vector.tensor_tensor(out=ss1[:], in0=s1a[:], in1=s1b[:], op=A.add)
                        vg(vector, vector.tensor_tensor(out=ss2[:], in0=s2a[:], in1=s2b[:], op=A.add))
                        vg(vector, vector.tensor_scalar(out=q2m[:], in0=ss2[:], scalar1=1.0 / D,
                                                        scalar2=None, op0=A.mult))
                        if dbg:
                            vector.tensor_copy(d_t5[:, 2:3], q2m[:])
                        s_vs.inc(vector.tensor_scalar(out=q2m[:], in0=q2m[:], scalar1=1e-6, scalar2=None,
                                                      op0=A.add))
                        vector.wait_ge(s_ri.h, tsub + 1)
                        vg(vector, vector.tensor_tensor(out=rr2[:], in0=rinv[:], in1=rinv[:], op=A.mult))
                        vg(vector, vector.tensor_scalar(out=mean2[:], in0=ss1[:], scalar1=1.0 / D, scalar2=None, op0=A.mult))
                        vg(vector, vector.tensor_tensor(out=mean2[:], in0=mean2[:], in1=rinv[:], op=A.mult))
                        vg(vector, vector.tensor_scalar(out=var2[:], in0=ss2[:], scalar1=1.0 / D, scalar2=None, op0=A.mult))
                        vg(vector, vector.tensor_tensor(out=var2[:], in0=var2[:], in1=rr2[:], op=A.mult))
                        vg(vector, vector.tensor_tensor(out=m22[:], in0=mean2[:], in1=mean2[:], op=A.mult))
                        vg(vector, vector.tensor_tensor(out=var2[:], in0=var2[:], in1=m22[:], op=A.subtract))
                        s_vv.inc(vector.tensor_scalar(out=var2[:], in0=var2[:], scalar1=EPS_LN,
                                                      scalar2=None, op0=A.add))
                        vector.wait_ge(s_rs.h, tsub + 1)
                        vg(vector, vector.tensor_tensor(out=alal[:], in0=rinv[:], in1=rstd2[:], op=A.mult))
                        vg(vector, vector.tensor_tensor(out=bebe[:], in0=mean2[:], in1=rstd2[:], op=A.mult))
                        vg(vector, vector.tensor_scalar(out=bebe[:], in0=bebe[:], scalar1=-1.0, scalar2=None, op0=A.mult))
                        if dbg:
                            if tsub >= 1:
                                vector.wait_ge(s_dt5s.h, 16 * tsub)
                            for ci, src in enumerate([ss1, ss2, rinv, rr2, mean2, var2, m22,
                                                      rstd2, alal, bebe, s1a, s2a]):
                                if ci != 2:
                                    vector.tensor_copy(d_t5[:, ci if ci < 2 else ci + 1:
                                                            (ci if ci < 2 else ci + 1) + 1], src[:])
                                else:
                                    vector.tensor_copy(d_t5[:, 3:4], src[:])
                            vector.tensor_copy(d_t5[:, 14:15], q2m[:])
                            vector.tensor_scalar(out=d_t5[:, 15:16], in0=ss2[:], scalar1=1.0 / D,
                                                 scalar2=None, op0=A.mult)
                            s_dt5.inc(vector.tensor_copy(d_t5[:, 13:14], q2m[:]))
                        if tsub >= 1:
                            vector.wait_ge(s_oo.h, 16 * tsub)
                        vector.tensor_scalar(out=ot[:, 0:1024], in0=y1t[:], scalar1=alal[:],
                                             scalar2=bebe[:], op0=A.mult, op1=A.add)
                        vg(vector, vector.tensor_scalar(out=ot[:, 1024:2048], in0=ysb[pq][:], scalar1=alal[:],
                                                        scalar2=bebe[:], op0=A.mult, op1=A.add))
                        s_vv.inc(vector.tensor_tensor(out=ot[:], in0=ot[:], in1=xpt[:], op=A.add))

                @blk.scalar
                def _(scalar):
                    for tsub in range(32):
                        pq = tsub % 2
                        scalar.wait_ge(s_ld1.h, 32 * (tsub + 1))
                        scalar.wait_ge(s_ye.h, 33 + tsub)
                        scalar.activation(out=jnk5[:], in_=y1t[:], func=AF.Square, accum_out=s2a[:])
                        scalar.activation(out=jnk5[:], in_=ysb[pq][:], func=AF.Square, accum_out=s2b[:])
                        scalar.activation(out=jnk5[:], in_=y1t[:], func=AF.Identity, accum_out=s1a[:])
                        s_ac.inc(scalar.activation(out=jnk5[:], in_=ysb[pq][:], func=AF.Identity,
                                                   accum_out=s1b[:]))
                        scalar.wait_ge(s_vs.h, tsub + 1)
                        scalar.activation(out=rinv[:], in_=q2m[:], func=AF.Ln)
                        s_ri.inc(scalar.activation(out=rinv[:], in_=rinv[:], func=AF.Exp, scale=-0.5))
                        scalar.wait_ge(s_vv.h, 2 * tsub + 1)
                        scalar.activation(out=rstd2[:], in_=var2[:], func=AF.Ln)
                        s_rs.inc(scalar.activation(out=rstd2[:], in_=rstd2[:], func=AF.Exp, scale=-0.5))

                @blk.gpsimd
                def _(g):
                    for tsub in range(32):
                        pq = tsub % 2
                        g.wait_ge(s_ye.h, tsub + 1)
                        sl = slice(tsub * 128, (tsub + 1) * 128)
                        s_y1s.inc(g.dma_start(out=y1_d[sl, :], in_=ysb[pq][:]), 16)
                    for tsub in range(32):
                        if dbg:
                            g.wait_ge(s_dt5.h, tsub + 1)
                            s_dt5s.inc(g.dma_start(out=dbg_t5[tsub * 128:(tsub + 1) * 128, :],
                                                   in_=d_t5[:]), 16)
                        g.wait_ge(s_vv.h, 2 * tsub + 2)
                        sl = slice(tsub * 128, (tsub + 1) * 128)
                        s_oo.inc(g.dma_start(out=out[sl, :], in_=ot[:]), 16)
                    g.wait_ge(s_y1s.h, s_y1s.n)
                    g.wait_ge(s_oo.h, s_oo.n)

    return nc


def _consts():
    idb = np.eye(128).astype(ml_dtypes.bfloat16)
    idf = np.eye(128, dtype=np.float32)
    jj, tt = np.meshgrid(np.arange(128), np.arange(128), indexing="ij")
    maskb = (jj <= tt).astype(ml_dtypes.bfloat16)
    onesb = np.ones((128, 128), ml_dtypes.bfloat16)
    onesf = np.ones((128, 128), np.float32)
    zerof = np.zeros((NH, L), np.float32)
    zero2 = np.zeros((128, DV), np.float32)
    return dict(idb=idb, idf=idf, maskb=maskb, onesb=onesb, onesf=onesf,
                zerof=zerof, zero2=zero2)


def make_inmaps(L0_attn_x, L0_attn_Wq, L0_attn_Wk, L0_attn_Wv, L0_attn_Wog, L0_attn_Wig,
                L0_attn_big, L0_attn_Wfg, L0_attn_bfg, L0_attn_gamma, L0_attn_Wout,
                L0_ffn_gate_up_packed, L0_ffn_gate_up_scales, L0_ffn_down_packed,
                L0_ffn_down_scales):
    cst = _consts()
    shared = dict(
        wq=np.asarray(L0_attn_Wq, np.float32),
        wk=np.asarray(L0_attn_Wk, np.float32),
        wv=np.asarray(L0_attn_Wv, np.float32),
        wog=np.asarray(L0_attn_Wog, np.float32),
        wig=np.asarray(L0_attn_Wig, np.float32),
        wfg=np.asarray(L0_attn_Wfg, np.float32),
        big=np.asarray(L0_attn_big, np.float32).reshape(NH, 1),
        bfg=np.asarray(L0_attn_bfg, np.float32).reshape(NH, 1),
        gamma=np.asarray(L0_attn_gamma, np.float32),
        wout=np.asarray(L0_attn_Wout, np.float32),
        gup=np.ascontiguousarray(np.asarray(L0_ffn_gate_up_packed, np.int32)).view(np.uint16),
        gsc=np.asarray(L0_ffn_gate_up_scales, np.float16),
        dwp=np.ascontiguousarray(np.asarray(L0_ffn_down_packed, np.int32)).view(np.uint16),
        dsc=np.asarray(L0_ffn_down_scales, np.float16),
        **cst,
    )
    xs = np.asarray(L0_attn_x, np.float32)
    return [dict(shared, x=np.ascontiguousarray(xs[b])) for b in range(8)]


def kernel(**inputs):
    nc = build_nc(dbg=False)
    in_maps = make_inmaps(**inputs)
    res = run_bass_kernel_spmd(nc, in_maps, core_ids=list(range(8)))
    return np.stack([np.asarray(r["out"]) for r in res.results], axis=0)
